# version 01 (final)
# baseline (speedup 1.0000x reference)
import contextlib
import numpy as np
import ml_dtypes
import concourse.bass as bass
import concourse.mybir as mybir
from concourse.bass_utils import run_bass_kernel_spmd

F32 = mybir.dt.float32
I32 = mybir.dt.int32
BF16 = mybir.dt.bfloat16
AF = mybir.ActivationFunctionType
ALU = mybir.AluOpType

B, L, D = 2, 4096, 2048
H, DK, NH = 8, 128, 2
EPS = 1e-5
W = 256
C = 128
TC = C // 2
PCH = W // TC
DFF = 8192
FT = 256
QTOK = L // 4

NCOL_QKV = 5120
NCOL = 6168
NCH_QKV = 40
NCH = 49

ENGINES = ("sync", "tensor", "vector", "scalar", "gpsimd")
PNC = [4, 5, 6, 7, 2, 3, 0, 1]


def consts_np():
    ident = np.eye(128, dtype=np.float32)
    m2 = np.where(np.arange(128)[None, :] >= np.arange(128)[:, None], 1e30, 0.0).astype(np.float32)
    allones = np.ones((128, 128), np.float32)
    dm2 = np.where(np.arange(128)[:, None] > (2 * np.arange(TC)[None, :] + 1), 1e30, 0.0).astype(np.float32)
    return np.concatenate([ident, m2, allones, dm2], axis=1)


class Sem:
    def __init__(self, handle):
        self.h = handle
        self.v = 0


class Prog:

    def __init__(self):
        self.ops = {e: [] for e in ENGINES}

    def do(self, eng, build, wait=(), inc=None, inc_n=1, self_inc=False):
        waits = [(s.h, s.v) for s in wait]
        tgt = None
        if inc is not None:
            inc.v += inc_n
            if not self_inc:
                tgt = (inc.h, inc_n)

        def f(e, waits=waits, build=build, tgt=tgt):
            for h, v in waits:
                if v > 0:
                    e.wait_ge(h, v)
            ins = build(e)
            if tgt is not None:
                assert ins is not None, "inc scheduled but build returned no instruction"
                ins.then_inc(tgt[0], tgt[1])
        self.ops[eng].append(f)

    def wait(self, eng, *sems):
        waits = [(s.h, s.v) for s in sems]

        def f(e, waits=waits):
            for h, v in waits:
                if v > 0:
                    e.wait_ge(h, v)
        self.ops[eng].append(f)


def build_program(debug=(), n_windows=L // W, do_attn=True, do_ffn=True, do_absmean=True, dbg_win=0, attn_stop='full', scan_depth=99, ffn_stop='full'):
    nc = bass.Bass("TRN2", target_bir_lowering=False)
    dbg_outs = {}

    def din(name, shape, dt=F32):
        return nc.dram_tensor(name, shape, dt, kind="ExternalInput").ap()

    def dbg(name, shape, dt=F32):
        if name in debug:
            dbg_outs[name] = nc.dram_tensor("dbg_" + name, shape, dt, kind="ExternalOutput").ap()
            return dbg_outs[name]
        return None

    xin = din("xin", [L, D])
    wcat = din("wcat", [D, NCOL])
    taps = din("taps", [128, NCH_QKV * 4])
    woin = din("woin", [1024, D])
    w1in = din("w1in", [DFF, D])
    w2in = din("w2in", [D, DFF])
    b1c = din("b1c", [128, DFF // 128])
    b2c = din("b2c", [128, D // 128])
    alog = din("alog", [H, 1])
    dtbin = din("dtbin", [H, 1])
    normw = din("normw", [128, 1])
    consts = din("consts", [128, 448])
    selq = din("selq", [128, 4])

    out = nc.dram_tensor("out", [QTOK, D], F32, kind="ExternalOutput").ap()

    w1q_scr = nc.dram_tensor("w1q_scr", [DFF, D], BF16, kind="Internal")
    w2q_scr = nc.dram_tensor("w2q_scr", [D, DFF], BF16, kind="Internal")
    x1_scr = nc.dram_tensor("x1_scr", [D, L], F32, kind="Internal")

    P = Prog()
    ctx = contextlib.ExitStack()
    with ctx:
        sb = lambda n, s, d=F32: ctx.enter_context(nc.sbuf_tensor(n, s, d))
        pst = lambda n, s, d=F32: ctx.enter_context(nc.psum_tensor(n, s, d))
        mksem = lambda n: Sem(ctx.enter_context(nc.semaphore(n)))

        cst = sb("cst", [128, 448])
        IDENT = cst[:, 0:128]
        M2 = cst[:, 128:256]
        ONES = cst[:, 256:384]
        DM2 = cst[:, 384:448]

        tapss = sb("tapss", [128, NCH_QKV * 4])
        selqs = sb("selqs", [128, 4])
        misc = sb("misc", [128, 8])
        b1s = sb("b1s", [128, DFF // 128])
        b2s = sb("b2s", [128, D // 128])

        am_ct = sb("am_ct", [128, 16 * 1024])
        am_ct2 = sb("am_ct2", [128, 16 * 1024])
        am_cmp = sb("am_cmp", [128, 16])
        am_tsc = sb("am_tsc", [16, 128])
        am_s2 = sb("am_s2", [32, 1024])
        am_s2o = am_ct[0:32, 2 * NH * H * W + H * 128:2 * NH * H * W + H * 128 + 1024]
        am_row = sb("am_row", [1, 80])
        qt_in = am_ct[:, 0:8192]
        qt_sg = am_ct[:, 8192:16384]
        qt_out = am_ct2[:, 0:4096].bitcast(BF16)

        ar2 = sb("ar2", [128, 8192])
        obuf = ar2[:, 0:H * W]
        xtile = ar2[:, H * W:H * W + 2048]
        flats = ar2[0:1, 6144:8192]
        cflat_p = flats[0:1, 0:1024]
        coddflat_p = flats[0:1, 1024:1536]
        ecoddflat_p = flats[0:1, 1536:2048]

        kb = am_ct[:, 0:NH * H * W]
        vb = am_ct[:, NH * H * W:2 * NH * H * W]
        xT = am_ct2[:, 0:16 * W]
        qb = am_ct2[:, 16 * W:16 * W + H * W]
        gateb = am_ct2[:, 16 * W + H * W:16 * W + 2 * H * W]

        wbuf = sb("wbuf", [128, 16 * 128])
        prebuf = sb("prebuf", [128, W + 8])
        convtmp = sb("convtmp", [128, W])
        sqtmp = sb("sqtmp", [128, W])
        tails = sb("tails", [128, NCH_QKV * 4])
        rowsb = sb("rowsb", [8, 3 * W])
        bsubt = sb("bsubt", [8, 2 * W])
        ldrow = sb("ldrow", [8, 2 * W])
        crow = sb("crow", [16, 2 * W])
        coddr = sb("coddr", [8, W])
        ecoddr = sb("ecoddr", [8, W])
        _o = 16 * W + 2 * H * W
        Mt = am_ct2[:, _o:_o + 128]; _o += 128
        Mpow = am_ct2[:, _o:_o + 768]; _o += 768
        MpowT = am_ct2[:, _o:_o + 768]; _o += 768
        Pt = am_ct2[:, _o:_o + 128]; _o += 128
        RHSt = am_ct2[:, _o:_o + 128]; _o += 128
        Ut = am_ct2[:, _o:_o + 128]; _o += 128
        Upt = am_ct2[:, _o:_o + 128]; _o += 128
        Vbt = am_ct2[:, _o:_o + 128]; _o += 128
        Etmp = am_ct2[:, _o:_o + 128]; _o += 128
        KrTt = am_ct2[:, _o:_o + 128]; _o += 128
        Dt = am_ct2[:, _o:_o + 64]; _o += 64
        DKQt = am_ct2[:, _o:_o + 64]; _o += 64
        QGt = am_ct2[:, _o:_o + 64]; _o += 64
        colb = am_ct2[:, _o:_o + 16]; _o += 16
        pcol = am_ct2[:, _o:_o + 40]; _o += 40
        lnr = am_ct2[0:1, _o:_o + 4 * W]; _o += 4 * W
        kcht = am_ct2[:, _o:_o + 128]; _o += 128
        vcht = am_ct2[:, _o:_o + 128]; _o += 128
        assert _o <= 16384
        Sst = am_ct[:, 2 * NH * H * W:2 * NH * H * W + H * 128]

        h_hi = am_ct[:, :].bitcast(BF16)
        h_lo = am_ct2[:, 0:8192].bitcast(BF16)
        x1t = am_ct2[:, 8192:12288]
        fwq = am_ct2[:, 12288:16384]
        x1sp = ar2[:, 0:6144].bitcast(BF16)

        psA = pst("psA", [128, 512])
        psB = pst("psB", [128, 512])
        psC = pst("psC", [128, 512])
        psD = pst("psD", [128, 512])
        psE = pst("psE", [128, 512])
        psF = pst("psF", [128, 512])
        psG = pst("psG", [128, 512])
        psH = pst("psH", [128, 512])

        sld = mksem("sld")
        slds = mksem("slds")
        sldg = mksem("sldg")
        spe = mksem("spe")
        sve = mksem("sve")
        sse = mksem("sse")
        spo = mksem("spo")
        sst = mksem("sst")

        d_thr = dbg("thr", [1, 80])
        d_s2 = dbg("s2", [32, 1024])
        d_wq1 = dbg("wq1", [128, D], BF16)
        d_wq2 = dbg("wq2", [128, DFF], BF16)

        def ld(e):
            for dd, ss in [(cst[:], consts), (tapss[:], taps), (misc[0:H, 1:2], dtbin),
                           (misc[:, 2:3], normw), (b1s[:], b1c), (b2s[:], b2c),
                           (selqs[:], selq), (misc[0:H, 0:1], alog)]:
                e.dma_start(out=dd, in_=ss).then_inc(sld.h, 16)
        P.do("sync", ld, inc=sld, inc_n=16 * 8, self_inc=True)

        P.do("scalar", lambda e: e.activation(misc[0:H, 0:1], misc[0:H, 0:1], AF.Exp), wait=[sld], inc=sse)
        P.do("vector", lambda e: e.tensor_scalar(misc[0:H, 0:1], misc[0:H, 0:1], -1.0, None, ALU.mult),
             wait=[sse], inc=sve)

        env = dict(locals())
        if do_absmean:
            _absmean(nc, P, env)
        if do_absmean:
            _quantize(nc, P, env)

        if do_attn:
            _attention(nc, P, env)
        if do_ffn:
            _ffn(nc, P, env)

        P.wait("gpsimd", sst)

        with nc.Block() as blk:
            @blk.sync
            def _(e):
                for f in P.ops["sync"]:
                    f(e)

            @blk.tensor
            def _(e):
                for f in P.ops["tensor"]:
                    f(e)

            @blk.vector
            def _(e):
                for f in P.ops["vector"]:
                    f(e)

            @blk.scalar
            def _(e):
                for f in P.ops["scalar"]:
                    f(e)

            @blk.gpsimd
            def _(e):
                for f in P.ops["gpsimd"]:
                    f(e)

    return nc, dbg_outs


def _absmean(nc, P, env):
    g = lambda k: env[k]
    w1in, w2in = g("w1in"), g("w2in")
    am_ct, am_ct2, am_cmp, am_tsc = g("am_ct"), g("am_ct2"), g("am_cmp"), g("am_tsc")
    am_s2, am_s2o, am_row, misc = g("am_s2"), g("am_s2o"), g("am_row"), g("misc")
    IDENT, ONES = g("IDENT"), g("ONES")
    psF = g("psF")
    sld, slds, spe, sve, sse = g("sld"), g("slds"), g("spe"), g("sve"), g("sse")
    d_thr = g("d_thr")

    def slab_src(s_i):
        if s_i < 8:
            return w1in[1024 * s_i:1024 * (s_i + 1), :]
        s = s_i - 8
        i, q = s // 4, s % 4
        return w2in[1024 * i:1024 * (i + 1), 2048 * q:2048 * (q + 1)]

    scans_done = [Sem(sve.h)]

    for s_i in range(16):
        src = slab_src(s_i)
        ct = am_ct if s_i % 2 == 0 else am_ct2

        def ctdma(e, src=src, ct=ct):
            for ee in range(2):
                for hi in range(16):
                    wr = hi * 2 + ee
                    sap = src[wr * 32:(wr + 1) * 32, :].rearrange("sr (wc sc) -> wc sr sc", sc=32)
                    dap = ct[64 * ee:64 * ee + 64, hi * 1024:(hi + 1) * 1024].rearrange(
                        "p (sr sc) -> p sr sc", sc=32)
                    e.dma_start(out=dap, in_=sap).then_inc(sld.h, 16)
        P.do("sync", ctdma, wait=[sve], inc=sld, inc_n=16 * 32, self_inc=True)

        def scans(e, ct=ct):
            e.tensor_scalar(ct[:].bitcast(I32), ct[:].bitcast(I32), 0x7FFFFFFF, None, ALU.bitwise_and)
            for hi in range(16):
                ins = e.tensor_tensor_scan(
                    ct[:, hi * 1024:(hi + 1) * 1024], ct[:, hi * 1024:(hi + 1) * 1024],
                    ct[:, hi * 1024:(hi + 1) * 1024], 0.0, ALU.add, ALU.bypass)
            return ins
        P.do("vector", scans, wait=[sld], inc=sve)
        P.do("vector", lambda e, ct=ct: e.tensor_copy(am_cmp[:], ct[:, 1023::1024]),
             wait=[sve, spe], inc=sve)

        P.do("tensor", lambda e: e.transpose(psF[0:16, 0:128], am_cmp[:, 0:16], IDENT),
             wait=[sve, sse, sld], inc=spe)

        P.do("scalar", lambda e: e.copy(am_tsc[:], psF[0:16, 0:128]), wait=[spe], inc=sse)

        def scatter(e, base=2 * s_i):
            for j in range(2):
                for wr in range(32):
                    e.dma_start(
                        out=am_s2[base + j:base + j + 1, wr * 32:(wr + 1) * 32],
                        in_=am_tsc[(wr >> 1):(wr >> 1) + 1,
                                   (wr & 1) * 64 + 32 * j:(wr & 1) * 64 + 32 * j + 32]).then_inc(slds.h, 16)
        P.do("scalar", scatter, wait=[sse], inc=slds, inc_n=16 * 64, self_inc=True)

    P.do("vector", lambda e: e.tensor_tensor_scan(am_s2o, am_s2[:], am_s2[:], 0.0, ALU.add, ALU.bypass),
         wait=[slds], inc=sve)
    P.do("gpsimd", lambda e: e.dma_start(
        out=am_row[0:1, 0:32], in_=am_s2o[:, 1023:1024]), wait=[sve], inc=sve, inc_n=16)

    r = am_row
    steps = [
        lambda e: e.tensor_tensor(r[0:1, 32:40], r[0:1, 0:16:2], r[0:1, 1:16:2], ALU.add),
        lambda e: e.tensor_tensor(r[0:1, 40:44], r[0:1, 32:36], r[0:1, 36:40], ALU.add),
        lambda e: e.tensor_tensor(r[0:1, 44:46], r[0:1, 40:42], r[0:1, 42:44], ALU.add),
        lambda e: e.tensor_tensor(r[0:1, 46:47], r[0:1, 44:45], r[0:1, 45:46], ALU.add),
        lambda e: e.tensor_scalar(r[0:1, 47:48], r[0:1, 46:47], float(np.float32(2.0 ** -24)), None, ALU.mult),
        lambda e: e.tensor_scalar(r[0:1, 48:49], r[0:1, 47:48], 0.7, None, ALU.mult),
        lambda e: e.tensor_tensor_scan(r[0:1, 56:64], r[0:1, 16:24], r[0:1, 16:24], 0.0, ALU.add, ALU.bypass),
        lambda e: e.tensor_tensor_scan(r[0:1, 64:72], r[0:1, 24:32], r[0:1, 24:32], 0.0, ALU.add, ALU.bypass),
        lambda e: e.tensor_tensor(r[0:1, 49:50], r[0:1, 63:64], r[0:1, 71:72], ALU.add),
        lambda e: e.tensor_scalar(r[0:1, 50:51], r[0:1, 49:50], float(np.float32(2.0 ** -24)), None, ALU.mult),
        lambda e: e.tensor_scalar(r[0:1, 51:52], r[0:1, 50:51], 0.7, None, ALU.mult),
    ]
    for st_ in steps:
        P.do("vector", st_, wait=[sve], inc=sve)

    def thrb(e):
        e.matmul(psF[:, 0:1], ONES[0:1, 0:128], am_row[0:1, 48:49], start=True, stop=True)
        return e.matmul(psF[:, 1:2], ONES[0:1, 0:128], am_row[0:1, 51:52], start=True, stop=True)
    P.do("tensor", thrb, wait=[sve], inc=spe)
    P.do("vector", lambda e: e.tensor_copy(misc[:, 3:5], psF[:, 0:2]), wait=[spe], inc=sve)

    if d_thr is not None:
        P.do("gpsimd", lambda e: e.dma_start(out=d_thr, in_=am_row[:]), wait=[sve], inc=g("sst"), inc_n=16)
    if g("d_s2") is not None:
        P.do("gpsimd", lambda e: e.dma_start(out=g("d_s2"), in_=am_s2[:]), wait=[sve], inc=g("sst"), inc_n=16)


def _quantize(nc, P, env):
    g = lambda k: env[k]
    w1in, w2in = g("w1in"), g("w2in")
    w1q_scr, w2q_scr = g("w1q_scr"), g("w2q_scr")
    qt_in, qt_sg, qt_out, misc = g("qt_in"), g("qt_sg"), g("qt_out"), g("misc")
    sld, sldg, sve, sse, spo = g("sld"), g("sldg"), g("sve"), g("sse"), g("spo")
    d_wq1, d_wq2 = g("d_wq1"), g("d_wq2")
    sst = g("sst")

    jobs = []
    for t in range(16):
        srcs = [w1in[512 * t + 128 * a:512 * t + 128 * (a + 1), :] for a in range(4)]
        dsts = [w1q_scr.ap()[512 * t + 128 * a:512 * t + 128 * (a + 1), :] for a in range(4)]
        jobs.append(("w1", srcs, dsts, 3))
    for t in range(16):
        jobs.append(("w2", [w2in[128 * t:128 * (t + 1), :]],
                     [w2q_scr.ap()[128 * t:128 * (t + 1), :]], 4))

    for ji, (mat, srcs, dsts, thrcol) in enumerate(jobs):
        def ldq(e, srcs=srcs):
            for a, s in enumerate(srcs):
                e.dma_start(out=qt_in[:, a * 2048:(a + 1) * 2048] if len(srcs) > 1 else qt_in,
                            in_=s).then_inc(sld.h, 16)
        P.do("sync", ldq, wait=[sve, spo], inc=sld, inc_n=16 * len(srcs), self_inc=True)
        P.do("scalar", lambda e: e.activation(qt_sg, qt_in, AF.Sign), wait=[sld], inc=sse)

        def quant(e, thrcol=thrcol):
            e.tensor_scalar(qt_in.bitcast(I32), qt_in.bitcast(I32), 0x7FFFFFFF, None, ALU.bitwise_and)
            e.tensor_scalar(qt_in, qt_in, misc[:, thrcol:thrcol + 1], None, ALU.is_ge)
            return e.tensor_tensor(qt_out, qt_in, qt_sg, ALU.mult)
        P.do("vector", quant, wait=[sse, sldg], inc=sve)
        def stq(e, dsts=dsts):
            for a, dd in enumerate(dsts):
                e.dma_start(out=dd, in_=qt_out[:, a * 2048:(a + 1) * 2048] if len(dsts) > 1 else qt_out
                            ).then_inc(sldg.h, 16)
        P.do("gpsimd", stq, wait=[sve], inc=sldg, inc_n=16 * len(dsts), self_inc=True)
        if ji == 0 and d_wq1 is not None:
            P.do("gpsimd", lambda e: e.dma_start(out=d_wq1, in_=qt_out[:, 0:2048]), wait=[sve], inc=sst, inc_n=16)
        if ji == 16 and d_wq2 is not None:
            P.do("gpsimd", lambda e: e.dma_start(out=d_wq2, in_=qt_out), wait=[sve], inc=sst, inc_n=16)
    P.do("gpsimd", lambda e: e.nop(), wait=[sldg], inc=spo)
    env["QUANT_DONE"] = Sem(spo.h)
    env["QUANT_DONE"].v = spo.v


def _attention(nc, P, env):
    g = env.__getitem__
    xin, wcat, woin, x1_scr = g("xin"), g("wcat"), g("woin"), g("x1_scr")
    cst, IDENT, M2, ONES, DM2 = g("cst"), g("IDENT"), g("M2"), g("ONES"), g("DM2")
    tapss, misc = g("tapss"), g("misc")
    kb, vb, qb, gateb, xT = g("kb"), g("vb"), g("qb"), g("gateb"), g("xT")
    obuf, xtile = g("obuf"), g("xtile")
    cflat_p, coddflat_p, ecoddflat_p = g("cflat_p"), g("coddflat_p"), g("ecoddflat_p")
    wbuf, prebuf, convtmp, sqtmp, tails = g("wbuf"), g("prebuf"), g("convtmp"), g("sqtmp"), g("tails")
    rowsb, bsubt, ldrow, crow, coddr, ecoddr, lnr = (g("rowsb"), g("bsubt"), g("ldrow"),
                                                     g("crow"), g("coddr"), g("ecoddr"), g("lnr"))
    colb, pcol = g("colb"), g("pcol")
    Mt, Mpow, MpowT, Pt, RHSt, Ut, Upt, Vbt, Etmp = (g("Mt"), g("Mpow"), g("MpowT"), g("Pt"),
                                                     g("RHSt"), g("Ut"), g("Upt"), g("Vbt"), g("Etmp"))
    Dt, DKQt, QGt, KrTt, Sst, kcht, vcht = (g("Dt"), g("DKQt"), g("QGt"), g("KrTt"), g("Sst"),
                                             g("kcht"), g("vcht"))
    psA, psB, psC, psD, psE, psF, psG, psH = (g("psA"), g("psB"), g("psC"), g("psD"),
                                              g("psE"), g("psF"), g("psG"), g("psH"))
    sld, slds, sldg = g("sld"), g("slds"), g("sldg")
    spe, sve, sse, spo, sst = g("spe"), g("sve"), g("sse"), g("spo"), g("sst")
    NWIN = env["n_windows"]
    dbg = env["dbg"]

    d_kqv = dbg("kqv", [128, 2 * NH * H * W + 2 * H * W])
    d_crw = dbg("crw", [16, 2 * W])
    d_o = dbg("o", [128, H * W])
    d_x1 = dbg("x1", [128, 16 * W])
    d_S = dbg("S", [128, H * 128])
    DBG_WIN = env.get("dbg_win", 0)

    RSQ = 1.0 / np.sqrt(128.0)

    def kv_ap(buf, h, p):
        return buf.rearrange("q (j h t) -> q h t j", j=2, h=8)[:, h, TC * p:TC * (p + 1), :]

    P.do("vector", lambda e: e.memset(Sst, 0.0), wait=[spo], inc=sve)
    P.do("vector", lambda e: e.memset(tails[:], 0.0), inc=sve)
    P.do("vector", lambda e: e.memset(misc[:, 5:6], 1.0), inc=sve)
    P.do("vector", lambda e: e.memset(misc[:, 6:7], EPS), inc=sve)
    P.do("vector", lambda e: e.memset(prebuf[:], 0.0), inc=sve)
    env["ATTN_INIT"] = sve.v

    for w in range(NWIN):
        T0 = w * W
        for r in range(W // 128):
            P.do("sync", lambda e, T0=T0, r=r: e.dma_start(out=xtile[:], in_=xin[T0 + 128 * r:T0 + 128 * (r + 1), :]),
                 wait=[sse, spo, sldg], inc=sld, inc_n=16)
            for kc in range(16):
                pf = psF if kc % 2 == 0 else psH
                P.do("tensor", lambda e, kc=kc, pf=pf: e.transpose(pf[:, 0:128], xtile[:, kc * 128:(kc + 1) * 128], IDENT),
                     wait=[sld, sse, sve], inc=spe)
                P.do("scalar", lambda e, kc=kc, r=r, pf=pf: e.copy(xT[:, kc * W + r * 128:kc * W + (r + 1) * 128], pf[:, 0:128]),
                     wait=[spe], inc=sse)

        crange = [(cc * 128, 128) for cc in range(48)] + [(6144, 8), (6152, 8), (6160, 8)]
        for cc, (c0, ncol) in enumerate(crange):
            wv = wbuf[:, 0:16 * ncol]

            def wld(e, c0=c0, ncol=ncol, wv=wv):
                return e.dma_start(out=wv.rearrange("p (kc c) -> p kc c", c=ncol),
                                  in_=wcat[:, c0:c0 + ncol].rearrange("(kc p) c -> p kc c", p=128))
            P.do("sync", wld, wait=[spe], inc=sld, inc_n=16)

            pp = psA if cc % 2 == 0 else psB

            def mms(e, wv=wv, pp=pp, ncol=ncol):
                ins = None
                for kc in range(16):
                    ins = e.matmul(pp[0:ncol, 0:W], wv[:, kc * ncol:(kc + 1) * ncol],
                                   xT[:, kc * W:(kc + 1) * W], start=(kc == 0), stop=(kc == 15))
                return ins
            P.do("tensor", mms, wait=[sld, sse, sve], inc=spe)

            if cc < 40:
                P.do("scalar", lambda e, pp=pp: e.copy(prebuf[:, 3:W + 3], pp[0:128, 0:W]), wait=[spe, sve], inc=sse)
                P.do("vector", lambda e, cc=cc: e.tensor_copy(prebuf[:, 0:3], tails[:, cc * 4:cc * 4 + 3]),
                     wait=[sse], inc=sve)
                P.do("vector", lambda e, cc=cc: e.tensor_copy(tails[:, cc * 4:cc * 4 + 3], prebuf[:, W:W + 3]),
                     wait=[sve], inc=sve)
                P.do("vector", lambda e, cc=cc: e.tensor_scalar(convtmp[:], prebuf[:, 0:W], tapss[:, cc * 4:cc * 4 + 1], None, ALU.mult),
                     wait=[sve], inc=sve)
                for dd in range(1, 4):
                    P.do("vector", lambda e, cc=cc, dd=dd: e.scalar_tensor_tensor(
                        convtmp[:], prebuf[:, dd:dd + W], tapss[:, cc * 4 + dd:cc * 4 + dd + 1],
                        convtmp[:], ALU.mult, ALU.add), wait=[sve], inc=sve)
                if cc < 8:
                    dest = qb[:, cc * W:(cc + 1) * W]
                elif cc < 24:
                    dest = kb[:, (cc - 8) * W:(cc - 8 + 1) * W]
                else:
                    dest = vb[:, (cc - 24) * W:(cc - 24 + 1) * W]
                P.do("scalar", lambda e, dest=dest: e.activation(dest, convtmp[:], AF.Silu), wait=[sve], inc=sse)
                if cc < 24:
                    P.do("scalar", lambda e, dest=dest: e.activation(sqtmp[:], dest, AF.Square), wait=[sse, spe], inc=sse)
                    P.do("tensor", lambda e: e.matmul(psG[0:1, 0:W], ONES[:, 0:1], sqtmp[:], start=True, stop=True),
                         wait=[sse, sve], inc=spe)
                    P.do("scalar", lambda e: e.activation(lnr[0:1, 0:W], psG[0:1, 0:W], AF.Sqrt), wait=[spe], inc=sse)
                    P.do("vector", lambda e: e.tensor_scalar(lnr[0:1, 0:W], lnr[0:1, 0:W], 1e-12, None, ALU.max),
                         wait=[sse], inc=sve)
                    P.do("vector", lambda e: e.reciprocal(lnr[0:1, 0:W], lnr[0:1, 0:W]), wait=[sve], inc=sve)
                    if cc < 8:
                        P.do("vector", lambda e: e.tensor_scalar(lnr[0:1, 0:W], lnr[0:1, 0:W], RSQ, None, ALU.mult),
                             wait=[sve], inc=sve)
                    P.do("tensor", lambda e: e.matmul(psH[:, 0:W], ONES[0:1, 0:128], lnr[0:1, 0:W], start=True, stop=True),
                         wait=[sve, sse], inc=spe)
                    P.do("vector", lambda e, dest=dest: e.tensor_tensor(dest, dest, psH[:, 0:W], ALU.mult),
                         wait=[spe], inc=sve)
            elif cc < 48:
                P.do("scalar", lambda e, cc=cc, pp=pp: e.copy(gateb[:, (cc - 40) * W:(cc - 40 + 1) * W], pp[0:128, 0:W]),
                     wait=[spe], inc=sse)
            elif cc == 48:
                P.do("scalar", lambda e, pp=pp: e.activation(bsubt[0:8, 0:2 * W:2], pp[0:8, 0:W], AF.Sigmoid),
                     wait=[spe], inc=sse)
            elif cc == 49:
                P.do("scalar", lambda e, pp=pp: e.activation(bsubt[0:8, 1:2 * W:2], pp[0:8, 0:W], AF.Sigmoid),
                     wait=[spe], inc=sse)
            else:
                P.do("vector", lambda e, pp=pp: e.tensor_scalar(rowsb[0:8, 0:W], pp[0:8, 0:W], misc[0:8, 1:2],
                                                                 None, ALU.add), wait=[spe], inc=sve)
                P.do("scalar", lambda e: e.nop(), wait=[sve], inc=sse)
                P.do("scalar", lambda e: e.activation(rowsb[0:8, W:2 * W], rowsb[0:8, 0:W], AF.Abs), wait=[sse], inc=sse)
                P.do("scalar", lambda e: e.activation(rowsb[0:8, W:2 * W], rowsb[0:8, W:2 * W], AF.Exp, scale=-1.0),
                     wait=[sse], inc=sse)
                P.do("scalar", lambda e: e.activation(rowsb[0:8, W:2 * W], rowsb[0:8, W:2 * W], AF.Ln, bias=misc[0:8, 5:6]),
                     wait=[sse], inc=sse)
                P.do("vector", lambda e: e.tensor_scalar(rowsb[0:8, 2 * W:3 * W], rowsb[0:8, 0:W], 0.0, None, ALU.max),
                     wait=[sse], inc=sve)
                P.do("vector", lambda e: e.tensor_tensor(rowsb[0:8, 2 * W:3 * W], rowsb[0:8, 2 * W:3 * W],
                                                         rowsb[0:8, W:2 * W], ALU.add), wait=[sve], inc=sve)
                P.do("vector", lambda e: e.tensor_scalar(rowsb[0:8, 2 * W:3 * W], rowsb[0:8, 2 * W:3 * W],
                                                         misc[0:8, 0:1], None, ALU.mult), wait=[sve], inc=sve)

        P.do("vector", lambda e: e.tensor_scalar(bsubt[:], bsubt[:], 2.0, None, ALU.mult), wait=[sse], inc=sve)
        P.do("gpsimd", lambda e: e.dma_start(out=crow[8:16, :], in_=bsubt[0:8, :]), wait=[sve], inc=sldg, inc_n=16)
        P.do("vector", lambda e: e.memset(ldrow[:], 0.0), inc=sve)
        P.do("vector", lambda e: e.tensor_copy(ldrow[0:8, 0:2 * W:2], rowsb[0:8, 2 * W:3 * W]), wait=[sve], inc=sve)
        for p in range(PCH):
            P.do("vector", lambda e, p=p: e.tensor_tensor_scan(
                crow[0:8, C * p:C * (p + 1)], ldrow[0:8, C * p:C * (p + 1)],
                ldrow[0:8, C * p:C * (p + 1)], 0.0, ALU.add, ALU.bypass), wait=[sve], inc=sve)
        P.do("vector", lambda e: e.tensor_copy(coddr[0:8, :], crow[0:8, 1:2 * W:2]), wait=[sve], inc=sve)
        P.do("scalar", lambda e: e.activation(ecoddr[0:8, :], coddr[0:8, :], AF.Exp), wait=[sve], inc=sse)
        if w == DBG_WIN and d_crw is not None:
            P.do("gpsimd", lambda e: e.dma_start(out=d_crw, in_=crow[:]), wait=[sve, sldg], inc=sst, inc_n=16)
        if w == DBG_WIN and d_kqv is not None:
            def dkqv(e):
                e.dma_start(out=d_kqv[:, 0:NH * H * W], in_=kb).then_inc(sst.h, 16)
                e.dma_start(out=d_kqv[:, NH * H * W:2 * NH * H * W], in_=vb).then_inc(sst.h, 16)
                e.dma_start(out=d_kqv[:, 2 * NH * H * W:2 * NH * H * W + H * W], in_=qb).then_inc(sst.h, 16)
                e.dma_start(out=d_kqv[:, 2 * NH * H * W + H * W:2 * NH * H * W + 2 * H * W], in_=gateb).then_inc(sst.h, 16)
            P.do("gpsimd", dkqv, wait=[sve, sse], inc=sst, inc_n=64, self_inc=True)

        if env["attn_stop"] == "kqv":
            continue
        for p in range(PCH):
            def flat(e, p=p):
                e.dma_start(out=cflat_p, in_=crow[0:8, C * p:C * (p + 1)]).then_inc(sldg.h, 16)
                e.dma_start(out=coddflat_p, in_=coddr[0:8, TC * p:TC * (p + 1)]).then_inc(sldg.h, 16)
                e.dma_start(out=ecoddflat_p, in_=ecoddr[0:8, TC * p:TC * (p + 1)]).then_inc(sldg.h, 16)
            P.do("gpsimd", flat, wait=[sve, sse, spe], inc=sldg, inc_n=48, self_inc=True)
            P.do("tensor", lambda e, p=p: e.transpose(psF[:, 0:16], crow[:, C * p:C * (p + 1)], IDENT[0:16, 0:16]),
                 wait=[sldg, sve, sse], inc=spe)
            P.do("scalar", lambda e: e.copy(colb[:], psF[:, 0:16]), wait=[spe], inc=sse)
            P.do("tensor", lambda e: e.matmul(psH[:, 0:8], ONES[0:1, 0:128], cflat_p[0:1, 127:1024:128],
                                              start=True, stop=True), wait=[sldg, sse], inc=spe)
            P.do("scalar", lambda e: e.activation(pcol[:, 0:8], colb[:, 0:8], AF.Exp), wait=[sse, spe], inc=sse)
            P.do("vector", lambda e: e.tensor_tensor(pcol[:, 8:16], colb[:, 8:16], pcol[:, 0:8], ALU.mult),
                 wait=[sse], inc=sve)
            P.do("vector", lambda e: e.tensor_scalar(pcol[:, 16:24], pcol[:, 8:16], -1.0, None, ALU.mult),
                 wait=[sve], inc=sve)
            P.do("vector", lambda e: e.tensor_tensor(pcol[:, 24:32], psH[:, 0:8], colb[:, 0:8], ALU.subtract),
                 wait=[spe, sse], inc=sve)
            P.do("scalar", lambda e: e.activation(pcol[:, 24:32], pcol[:, 24:32], AF.Exp), wait=[sve], inc=sse)
            P.do("scalar", lambda e: e.activation(pcol[:, 32:40], psH[:, 0:8], AF.Exp), wait=[sve], inc=sse)

            for h in range(H):
                kap = kv_ap(kb, h, p)
                vap = kv_ap(vb, h, p)
                ccol = colb[:, h:h + 1]
                bcol = colb[:, 8 + h:9 + h]
                P.do("vector", lambda e, kap=kap: e.tensor_copy(kcht, kap), wait=[sve, spe], inc=sve)
                P.do("tensor", lambda e: e.matmul(psD[:, 0:128], kcht, kcht, start=True, stop=True),
                     wait=[sse, sve], inc=spe)
                P.do("tensor", lambda e, h=h: e.matmul(psD[:, 128:256], ONES[0:1, 0:128],
                                                       cflat_p[0:1, h * 128:(h + 1) * 128], start=True, stop=True),
                     inc=spe)
                P.do("vector", lambda e, ccol=ccol: e.scalar_tensor_tensor(
                    Etmp, psD[:, 128:256], ccol, M2, ALU.subtract, ALU.add), wait=[spe], inc=sve)
                P.do("scalar", lambda e: e.activation(Etmp, Etmp, AF.Exp, scale=-1.0), wait=[sve], inc=sse)
                P.do("vector", lambda e, bcol=bcol: e.scalar_tensor_tensor(
                    Mt, Etmp, bcol, psD[:, 0:128], ALU.mult, ALU.mult), wait=[sse], inc=sve)
                if env["scan_depth"] < 2:
                    continue
                P.do("tensor", lambda e: e.transpose(psD[:, 256:384], Mt, IDENT), wait=[sve], inc=spe)
                P.do("scalar", lambda e: e.copy(MpowT[:, 0:128], psD[:, 256:384]), wait=[spe], inc=sse)
                prev_n = Mt
                for lv in range(6):
                    P.do("tensor", lambda e, lv=lv, prev_n=prev_n: e.matmul(
                        psD[:, 384:512], MpowT[:, lv * 128:(lv + 1) * 128], prev_n,
                        start=True, stop=True), wait=[sse, sve], inc=spe)
                    P.do("scalar", lambda e, lv=lv: e.copy(Mpow[:, lv * 128:(lv + 1) * 128], psD[:, 384:512]),
                         wait=[spe], inc=sse)
                    if lv < 5:
                        P.do("tensor", lambda e, lv=lv: e.transpose(psD[:, 256:384], Mpow[:, lv * 128:(lv + 1) * 128], IDENT),
                             wait=[sse], inc=spe)
                        P.do("scalar", lambda e, lv=lv: e.copy(MpowT[:, (lv + 1) * 128:(lv + 2) * 128], psD[:, 256:384]),
                             wait=[spe], inc=sse)
                    prev_n = Mpow[:, lv * 128:(lv + 1) * 128]
                P.do("vector", lambda e: e.tensor_tensor(Pt, IDENT, MpowT[:, 0:128], ALU.subtract),
                     wait=[sse], inc=sve)
                for pw in range(6):
                    P.do("tensor", lambda e, pw=pw: e.matmul(psE[:, 0:128], Mpow[:, pw * 128:(pw + 1) * 128], Pt,
                                                             start=True, stop=True), wait=[sve], inc=spe)
                    P.do("vector", lambda e: e.tensor_tensor(Pt, Pt, psE[:, 0:128], ALU.add),
                         wait=[spe], inc=sve)
                if env["scan_depth"] < 3:
                    continue
                P.do("tensor", lambda e, h=h: e.matmul(psE[:, 128:256], kcht, Sst[:, h * 128:(h + 1) * 128],
                                                       start=True, stop=True), wait=[sve], inc=spe)
                P.do("vector", lambda e, vap=vap: e.tensor_copy(vcht, vap), wait=[sve, spe], inc=sve)
                P.do("tensor", lambda e: e.transpose(psF[:, 128:256], vcht, IDENT), wait=[sse, sve], inc=spe)
                P.do("vector", lambda e, bcol=bcol: e.tensor_scalar(Vbt, psF[:, 128:256], bcol, None, ALU.mult),
                     wait=[spe], inc=sve)
                P.do("vector", lambda e, h=h: e.scalar_tensor_tensor(
                    RHSt, psE[:, 128:256], pcol[:, 16 + h:17 + h], Vbt, ALU.mult, ALU.add),
                     wait=[sve, spe], inc=sve)
                P.do("tensor", lambda e: e.matmul(psE[:, 256:384], Pt, RHSt, start=True, stop=True),
                     wait=[sve], inc=spe)
                P.do("scalar", lambda e: e.copy(Ut, psE[:, 256:384]), wait=[spe], inc=sse)
                P.do("vector", lambda e, h=h: e.tensor_scalar(Upt, Ut, pcol[:, 24 + h:25 + h], None, ALU.mult),
                     wait=[sse], inc=sve)
                if env["scan_depth"] < 4:
                    continue
                qap = qb[:, h * W + TC * p: h * W + TC * (p + 1)]
                P.do("tensor", lambda e, qap=qap: e.matmul(psF[:, 256:320], kcht, qap, start=True, stop=True),
                     wait=[sve], inc=spe)
                P.do("tensor", lambda e, h=h: e.matmul(psF[:, 320:384], ONES[0:1, 0:128],
                                                       coddflat_p[0:1, h * 64:(h + 1) * 64], start=True, stop=True),
                     inc=spe)
                P.do("vector", lambda e, ccol=ccol: e.scalar_tensor_tensor(
                    Dt, psF[:, 320:384], ccol, DM2, ALU.subtract, ALU.subtract), wait=[spe], inc=sve)
                P.do("scalar", lambda e: e.activation(Dt, Dt, AF.Exp), wait=[sve], inc=sse)
                P.do("vector", lambda e: e.tensor_tensor(DKQt, Dt, psF[:, 256:320], ALU.mult),
                     wait=[sse], inc=sve)
                P.do("tensor", lambda e, h=h: e.matmul(psF[:, 384:448], ONES[0:1, 0:128],
                                                       ecoddflat_p[0:1, h * 64:(h + 1) * 64], start=True, stop=True),
                     wait=[sve], inc=spe)
                P.do("vector", lambda e, qap=qap: e.tensor_tensor(QGt, qap, psF[:, 384:448], ALU.mult),
                     wait=[spe], inc=sve)
                P.do("tensor", lambda e: e.matmul(psE[:, 384:448], Ut, DKQt, start=True, stop=False),
                     wait=[sve, sse], inc=spe)
                P.do("tensor", lambda e, h=h: e.matmul(psE[:, 384:448], Sst[:, h * 128:(h + 1) * 128], QGt,
                                                       start=False, stop=True), wait=[sve], inc=spe)
                P.do("scalar", lambda e, h=h, p=p: e.copy(obuf[:, h * W + TC * p:h * W + TC * (p + 1)],
                                                          psE[:, 384:448]), wait=[spe], inc=sse)
                if env["scan_depth"] < 5:
                    continue
                P.do("tensor", lambda e: e.transpose(psF[:, 0:128], kcht, IDENT), wait=[sve], inc=spe)
                P.do("scalar", lambda e: e.copy(KrTt, psF[:, 0:128]), wait=[spe], inc=sse)
                P.do("tensor", lambda e: e.matmul(psC[:, 0:128], KrTt, Upt, start=True, stop=True),
                     wait=[sse, sve], inc=spe)
                P.do("vector", lambda e, h=h: e.scalar_tensor_tensor(
                    Sst[:, h * 128:(h + 1) * 128], Sst[:, h * 128:(h + 1) * 128], pcol[:, 32 + h:33 + h],
                    psC[:, 0:128], ALU.mult, ALU.add), wait=[spe], inc=sve)

        if w == DBG_WIN and d_S is not None:
            P.do("gpsimd", lambda e: e.dma_start(out=d_S, in_=Sst), wait=[sve], inc=sst, inc_n=16)
        if env["attn_stop"] == "scan":
            continue

        for h in range(H):
            osl = obuf[:, h * W:(h + 1) * W]
            P.do("scalar", lambda e, osl=osl: e.activation(sqtmp[:], osl, AF.Square, scale=RSQ), wait=[sse, sve, spe], inc=sse)
            P.do("tensor", lambda e: e.matmul(psG[0:1, 0:W], ONES[:, 0:1], sqtmp[:], start=True, stop=True),
                 wait=[sse, sve], inc=spe)
            P.do("scalar", lambda e: e.activation(lnr[0:1, 0:W], psG[0:1, 0:W], AF.Sqrt,
                                                  bias=misc[0:1, 6:7]), wait=[spe], inc=sse)
            P.do("vector", lambda e: e.reciprocal(lnr[0:1, 0:W], lnr[0:1, 0:W]), wait=[sse], inc=sve)
            P.do("tensor", lambda e: e.matmul(psH[:, 0:W], ONES[0:1, 0:128], lnr[0:1, 0:W], start=True, stop=True),
                 wait=[sve], inc=spe)
            P.do("vector", lambda e, osl=osl: e.scalar_tensor_tensor(osl, osl, misc[:, 2:3], psH[:, 0:W],
                                                                     ALU.mult, ALU.mult), wait=[spe], inc=sve)
            P.do("scalar", lambda e, h=h: e.activation(convtmp[:], gateb[:, h * W:(h + 1) * W], AF.Silu),
                 wait=[sve], inc=sse)
            P.do("vector", lambda e, osl=osl: e.tensor_tensor(osl, osl, convtmp[:], ALU.mult),
                 wait=[sse], inc=sve)
        if w == DBG_WIN and d_o is not None:
            P.do("gpsimd", lambda e: e.dma_start(out=d_o, in_=obuf[:, 0:H * W]), wait=[sve], inc=sst, inc_n=16)

        for dc in range(16):
            wv = wbuf[:, 0:8 * 128]
            P.do("sync", lambda e, dc=dc, wv=wv: e.dma_start(
                out=wv.rearrange("p (fc c) -> p fc c", c=128),
                in_=woin[:, dc * 128:(dc + 1) * 128].rearrange("(fc p) c -> p fc c", p=128)),
                 wait=[spe], inc=sld, inc_n=16)
            pp = psA if dc % 2 == 0 else psB

            def womm(e, wv=wv, pp=pp):
                ins = None
                for fc in range(8):
                    ins = e.matmul(pp[:, 0:W], wv[:, fc * 128:(fc + 1) * 128],
                                   obuf[:, fc * W:(fc + 1) * W], start=(fc == 0), stop=(fc == 7))
                return ins
            P.do("tensor", womm, wait=[sld, sve], inc=spe)
            P.do("vector", lambda e, dc=dc, pp=pp: e.tensor_tensor(
                xT[:, dc * W:(dc + 1) * W], xT[:, dc * W:(dc + 1) * W], pp[:, 0:W], ALU.add),
                 wait=[spe], inc=sve)

        for kc in range(16):
            P.do("scalar", lambda e, kc=kc: e.activation(sqtmp[:], xT[:, kc * W:(kc + 1) * W], AF.Square),
                 wait=[sve, spe], inc=sse)
            P.do("tensor", lambda e, kc=kc: e.matmul(psG[0:1, 0:W], ONES[:, 0:1], xT[:, kc * W:(kc + 1) * W],
                                                     start=(kc == 0), stop=(kc == 15)), wait=[sve, sse], inc=spe)
            P.do("tensor", lambda e, kc=kc: e.matmul(psH[0:1, 0:W], ONES[:, 0:1], sqtmp[:],
                                                     start=(kc == 0), stop=(kc == 15)), wait=[sse], inc=spe)
        P.do("scalar", lambda e: e.copy(lnr[0:1, 0:W], psG[0:1, 0:W]), wait=[spe], inc=sse)
        P.do("scalar", lambda e: e.copy(lnr[0:1, W:2 * W], psH[0:1, 0:W]), wait=[spe], inc=sse)
        P.do("vector", lambda e: e.tensor_scalar(lnr[0:1, 0:W], lnr[0:1, 0:W], 1.0 / D, None, ALU.mult),
             wait=[sse], inc=sve)
        P.do("vector", lambda e: e.tensor_scalar(lnr[0:1, W:2 * W], lnr[0:1, W:2 * W], 1.0 / D, None, ALU.mult),
             wait=[sve], inc=sve)
        P.do("vector", lambda e: e.tensor_tensor(lnr[0:1, 3 * W:4 * W], lnr[0:1, 0:W], lnr[0:1, 0:W], ALU.mult),
             wait=[sve], inc=sve)
        P.do("vector", lambda e: e.tensor_tensor(lnr[0:1, W:2 * W], lnr[0:1, W:2 * W], lnr[0:1, 3 * W:4 * W],
                                                 ALU.subtract), wait=[sve], inc=sve)
        P.do("scalar", lambda e: e.activation(lnr[0:1, W:2 * W], lnr[0:1, W:2 * W], AF.Sqrt,
                                              bias=misc[0:1, 6:7]), wait=[sve], inc=sse)
        P.do("vector", lambda e: e.reciprocal(lnr[0:1, W:2 * W], lnr[0:1, W:2 * W]), wait=[sse], inc=sve)
        P.do("vector", lambda e: e.tensor_tensor(lnr[0:1, 2 * W:3 * W], lnr[0:1, 0:W], lnr[0:1, W:2 * W],
                                                 ALU.mult), wait=[sve], inc=sve)
        P.do("vector", lambda e: e.tensor_scalar(lnr[0:1, 2 * W:3 * W], lnr[0:1, 2 * W:3 * W], -1.0, None,
                                                 ALU.mult), wait=[sve], inc=sve)
        P.do("tensor", lambda e: e.matmul(psG[:, 0:W], ONES[0:1, 0:128], lnr[0:1, W:2 * W], start=True, stop=True),
             wait=[sve], inc=spe)
        P.do("tensor", lambda e: e.matmul(psH[:, 0:W], ONES[0:1, 0:128], lnr[0:1, 2 * W:3 * W], start=True, stop=True),
             wait=[sve], inc=spe)
        for kc in range(16):
            P.do("vector", lambda e, kc=kc: e.tensor_tensor(xT[:, kc * W:(kc + 1) * W], xT[:, kc * W:(kc + 1) * W],
                                                            psG[:, 0:W], ALU.mult), wait=[spe], inc=sve)
            P.do("vector", lambda e, kc=kc: e.tensor_tensor(xT[:, kc * W:(kc + 1) * W], xT[:, kc * W:(kc + 1) * W],
                                                            psH[:, 0:W], ALU.add), wait=[spe], inc=sve)
        if w == DBG_WIN and d_x1 is not None:
            P.do("gpsimd", lambda e: e.dma_start(out=d_x1, in_=xT[:, 0:16 * W]), wait=[sve], inc=sst, inc_n=16)

        def x1st(e, T0=T0):
            for kc in range(16):
                e.dma_start(out=x1_scr.ap()[kc * 128:(kc + 1) * 128, T0:T0 + W],
                            in_=xT[:, kc * W:(kc + 1) * W]).then_inc(sldg.h, 16)
        P.do("gpsimd", x1st, wait=[sve], inc=sldg, inc_n=16 * 16, self_inc=True)

    env["ATTN_DONE_SLDG"] = sldg.v


def _ffn(nc, P, env):
    g = env.__getitem__
    x1_scr, w1q_scr, w2q_scr, out = g("x1_scr"), g("w1q_scr"), g("w2q_scr"), g("out")
    am_ct, am_ct2 = g("am_ct"), g("am_ct2")
    h_hi, h_lo, x1t, fwq, x1sp = g("h_hi"), g("h_lo"), g("x1t"), g("fwq"), g("x1sp")
    b1s, b2s, misc = g("b1s"), g("b2s"), g("misc")
    IDENT, ONES = g("IDENT"), g("ONES")
    convtmp, sqtmp, selqs = g("convtmp"), g("sqtmp"), g("selqs")
    lnr = g("ar2")[0:1, 6144:7168]
    psA, psB, psC, psD, psE, psF, psG, psH = (g("psA"), g("psB"), g("psC"), g("psD"),
                                              g("psE"), g("psF"), g("psG"), g("psH"))
    sld, slds, sldg = g("sld"), g("slds"), g("sldg")
    spe, sve, sse, spo, sst = g("spe"), g("sve"), g("sse"), g("spo"), g("sst")
    dbg = env["dbg"]

    fwq_bf = fwq.bitcast(BF16)
    xsp_tmp = am_ct[:, 12288:16384]
    outb = am_ct2[:, 0:4096]
    NFB = QTOK // FT
    d_h = dbg("hff", [128, 4 * FT], BF16)
    d_y2 = dbg("y2", [128, 16 * FT])
    h_mid = h_hi

    for k in range(NFB):
        P.do("vector", lambda e: e.memset(x1t, 0.0), wait=[sve, spe, sse], inc=sve)
        for qq in range(4):
            P.do("gpsimd", lambda e, qq=qq, k=k: e.dma_start(
                out=xsp_tmp.rearrange("p (dc t) -> p dc t", t=FT),
                in_=x1_scr.ap()[:, QTOK * qq + FT * k:QTOK * qq + FT * (k + 1)].rearrange(
                    "(dc p) t -> p dc t", p=128)), wait=[sldg, spo, sve], inc=sldg, inc_n=16)
            P.do("vector", lambda e, qq=qq: e.scalar_tensor_tensor(
                x1t, xsp_tmp, selqs[:, qq:qq + 1], x1t, ALU.mult, ALU.add), wait=[sldg, sve], inc=sve)

        P.do("vector", lambda e: e.tensor_copy(x1sp[:, 0:4096], x1t), wait=[sldg, spe], inc=sve)
        P.do("vector", lambda e: e.tensor_tensor(xsp_tmp, x1t, x1sp[:, 0:4096], ALU.subtract),
             wait=[sve], inc=sve)
        P.do("vector", lambda e: e.tensor_copy(x1sp[:, 4096:8192], xsp_tmp), wait=[sve], inc=sve)
        P.do("vector", lambda e: e.tensor_tensor(x1sp[:, 8192:12288], xsp_tmp, x1sp[:, 4096:8192],
                                                 ALU.subtract), wait=[sve], inc=sve)

        if env["ffn_stop"] == "ldx1":
            continue
        for fs in range(16):
            def wq1(e, fs=fs):
                for dc in range(16):
                    e.dma_start(out=fwq_bf[:, dc * 512:(dc + 1) * 512],
                                in_=w1q_scr.ap()[fs * 512:(fs + 1) * 512, dc * 128:(dc + 1) * 128],
                                transpose=True).then_inc(slds.h, 16)
            P.do("scalar", wq1, wait=[spe, sve], inc=slds, inc_n=16 * 16, self_inc=True)
            for fi in range(4):
                fc = fs * 4 + fi
                pp = psA if fc % 2 == 0 else psB

                def mm1(e, fi=fi, pp=pp):
                    ins = None
                    for dc in range(16):
                        for sp in range(3):
                            ins = e.matmul(pp[:, 0:FT], fwq_bf[:, dc * 512 + fi * 128:dc * 512 + (fi + 1) * 128],
                                           x1sp[:, sp * 4096 + dc * FT:sp * 4096 + (dc + 1) * FT],
                                           start=(dc == 0 and sp == 0), stop=(dc == 15 and sp == 2))
                    return ins
                P.do("tensor", mm1, wait=[slds, sve, sse], inc=spe)
                P.do("scalar", lambda e, fc=fc, pp=pp: e.activation(convtmp[:, 0:FT], pp[:, 0:FT], AF.Gelu,
                                                                    bias=b1s[:, fc:fc + 1]), wait=[spe], inc=sse)
                P.do("vector", lambda e, fc=fc: e.tensor_copy(h_hi[:, fc * FT:(fc + 1) * FT], convtmp[:, 0:FT]),
                     wait=[sse], inc=sve)
                P.do("vector", lambda e, fc=fc: e.tensor_tensor(sqtmp[:, 0:FT], convtmp[:, 0:FT],
                                                                h_hi[:, fc * FT:(fc + 1) * FT], ALU.subtract),
                     wait=[sve], inc=sve)
                P.do("vector", lambda e, fc=fc: e.tensor_copy(h_mid[:, 16384 + fc * FT:16384 + (fc + 1) * FT],
                                                              sqtmp[:, 0:FT]), wait=[sve], inc=sve)
                P.do("vector", lambda e, fc=fc: e.tensor_tensor(h_lo[:, fc * FT:(fc + 1) * FT], sqtmp[:, 0:FT],
                                                                h_mid[:, 16384 + fc * FT:16384 + (fc + 1) * FT],
                                                                ALU.subtract), wait=[sve], inc=sve)

        if k == 0 and d_h is not None:
            P.do("gpsimd", lambda e: e.dma_start(out=d_h, in_=h_hi[:, 0:4 * FT]),
                 wait=[sve], inc=sst, inc_n=16)

        if env["ffn_stop"] == "mm1":
            continue
        for ds in range(4):
            for fc in range(64):
                slot = fc % 2

                def wq2(e, ds=ds, fc=fc, slot=slot):
                    e.dma_start(out=fwq_bf[:, slot * 512:(slot + 1) * 512],
                                in_=w2q_scr.ap()[ds * 512:(ds + 1) * 512, fc * 128:(fc + 1) * 128],
                                transpose=True).then_inc(slds.h, 16)
                P.do("scalar", wq2, wait=[spe], inc=slds, inc_n=16, self_inc=True)

                def mm2(e, ds=ds, fc=fc, slot=slot):
                    ins = None
                    for di in range(4):
                        pp = (psA, psC, psD, psE)[di]
                        po = 0
                        for sp in range(3):
                            hs = (h_hi[:, fc * FT:(fc + 1) * FT] if sp == 0 else
                                  h_mid[:, 16384 + fc * FT:16384 + (fc + 1) * FT] if sp == 1 else
                                  h_lo[:, fc * FT:(fc + 1) * FT])
                            ins = e.matmul(pp[:, po:po + FT],
                                           fwq_bf[:, slot * 512 + di * 128:slot * 512 + (di + 1) * 128],
                                           hs, start=(fc == 0 and sp == 0), stop=(fc == 63 and sp == 2))
                    return ins
                P.do("tensor", mm2, wait=[slds, sve], inc=spe)
            for di in range(4):
                dc = ds * 4 + di
                pp = (psA, psC, psD, psE)[di]
                po = 0
                P.do("vector", lambda e, dc=dc, pp=pp, po=po: e.scalar_tensor_tensor(
                    x1t[:, dc * FT:(dc + 1) * FT], pp[:, po:po + FT], b2s[:, dc:dc + 1],
                    x1t[:, dc * FT:(dc + 1) * FT], ALU.add, ALU.add), wait=[spe], inc=sve)

        if k == 0 and d_y2 is not None:
            P.do("gpsimd", lambda e: e.dma_start(out=d_y2, in_=x1t), wait=[sve], inc=sst, inc_n=16)

        if env["ffn_stop"] == "mm2":
            continue
        for dc in range(16):
            P.do("scalar", lambda e, dc=dc: e.activation(sqtmp[:, 0:FT], x1t[:, dc * FT:(dc + 1) * FT], AF.Square),
                 wait=[sve, spe], inc=sse)
            P.do("tensor", lambda e, dc=dc: e.matmul(psG[0:1, 0:FT], ONES[:, 0:1], x1t[:, dc * FT:(dc + 1) * FT],
                                                     start=(dc == 0), stop=(dc == 15)), wait=[sve, sse], inc=spe)
            P.do("tensor", lambda e, dc=dc: e.matmul(psH[0:1, 0:FT], ONES[:, 0:1], sqtmp[:, 0:FT],
                                                     start=(dc == 0), stop=(dc == 15)), wait=[sse], inc=spe)
        P.do("scalar", lambda e: e.copy(lnr[0:1, 0:FT], psG[0:1, 0:FT]), wait=[spe], inc=sse)
        P.do("scalar", lambda e: e.copy(lnr[0:1, FT:2 * FT], psH[0:1, 0:FT]), wait=[spe], inc=sse)
        P.do("vector", lambda e: e.tensor_scalar(lnr[0:1, 0:FT], lnr[0:1, 0:FT], 1.0 / D, None, ALU.mult),
             wait=[sse], inc=sve)
        P.do("vector", lambda e: e.tensor_scalar(lnr[0:1, W:W + FT], lnr[0:1, W:W + FT], 1.0 / D, None, ALU.mult),
             wait=[sve], inc=sve)
        P.do("vector", lambda e: e.tensor_tensor(lnr[0:1, 3 * W:3 * W + FT], lnr[0:1, 0:FT], lnr[0:1, 0:FT],
                                                 ALU.mult), wait=[sve], inc=sve)
        P.do("vector", lambda e: e.tensor_tensor(lnr[0:1, W:W + FT], lnr[0:1, W:W + FT],
                                                 lnr[0:1, 3 * W:3 * W + FT], ALU.subtract), wait=[sve], inc=sve)
        P.do("scalar", lambda e: e.activation(lnr[0:1, W:W + FT], lnr[0:1, W:W + FT], AF.Sqrt,
                                              bias=misc[0:1, 6:7]), wait=[sve], inc=sse)
        P.do("vector", lambda e: e.reciprocal(lnr[0:1, W:W + FT], lnr[0:1, W:W + FT]), wait=[sse], inc=sve)
        P.do("vector", lambda e: e.tensor_tensor(lnr[0:1, 2 * W:2 * W + FT], lnr[0:1, 0:FT],
                                                 lnr[0:1, W:W + FT], ALU.mult), wait=[sve], inc=sve)
        P.do("vector", lambda e: e.tensor_scalar(lnr[0:1, 2 * W:2 * W + FT], lnr[0:1, 2 * W:2 * W + FT], -1.0,
                                                 None, ALU.mult), wait=[sve], inc=sve)
        P.do("tensor", lambda e: e.matmul(psG[:, 0:FT], ONES[0:1, 0:128], lnr[0:1, W:W + FT],
                                          start=True, stop=True), wait=[sve], inc=spe)
        P.do("tensor", lambda e: e.matmul(psH[:, 0:FT], ONES[0:1, 0:128], lnr[0:1, 2 * W:2 * W + FT],
                                          start=True, stop=True), wait=[sve], inc=spe)
        for dc in range(16):
            P.do("vector", lambda e, dc=dc: e.tensor_tensor(x1t[:, dc * FT:(dc + 1) * FT],
                                                            x1t[:, dc * FT:(dc + 1) * FT], psG[:, 0:FT],
                                                            ALU.mult), wait=[spe], inc=sve)
            P.do("vector", lambda e, dc=dc: e.tensor_tensor(x1t[:, dc * FT:(dc + 1) * FT],
                                                            x1t[:, dc * FT:(dc + 1) * FT], psH[:, 0:FT],
                                                            ALU.add), wait=[spe], inc=sve)

        for dc in range(16):
            for r in range(2):
                pf = psF if (dc * 2 + r) % 2 == 0 else psH
                P.do("tensor", lambda e, dc=dc, r=r, pf=pf: e.transpose(
                    pf[:, 128:256], x1t[:, dc * FT + r * 128:dc * FT + (r + 1) * 128], IDENT),
                     wait=[sve, sse], inc=spe)
                P.do("scalar", lambda e, dc=dc, r=r, pf=pf: e.copy(outb[:, r * 2048 + dc * 128:r * 2048 + (dc + 1) * 128],
                                                                   pf[:, 128:256]), wait=[spe], inc=sse)

        def ost(e, k=k):
            for r in range(2):
                e.dma_start(out=out[FT * k + r * 128:FT * k + (r + 1) * 128, :],
                            in_=outb[:, r * 2048:(r + 1) * 2048]).then_inc(sst.h, 16)
        P.do("gpsimd", ost, wait=[sse], inc=sst, inc_n=32, self_inc=True)


def _prep_inputs(inputs):
    f32 = np.float32
    Wq = inputs["L0_attn_Wq"]; Wk = inputs["L0_attn_Wk"]; Wv = inputs["L0_attn_Wv"]
    Wb = inputs["L0_attn_Wb"]; Wa = inputs["L0_attn_Wa"]; Wg = inputs["L0_attn_Wg"]
    wcat = np.concatenate([Wq, Wk, Wv, Wg, Wb, Wa], axis=1).astype(f32)
    assert wcat.shape == (D, NCOL)
    cq = inputs["L0_attn_conv_q_w"][:, 0, :]
    ck = inputs["L0_attn_conv_k_w"][:, 0, :]
    cv = inputs["L0_attn_conv_v_w"][:, 0, :]
    tap = np.concatenate([cq, ck, cv], axis=0).astype(f32)
    taps = np.zeros((128, NCH_QKV * 4), f32)
    for ch in range(NCH_QKV):
        taps[:, ch * 4:(ch + 1) * 4] = tap[ch * 128:(ch + 1) * 128, :]
    com = {
        "wcat": wcat,
        "taps": taps,
        "woin": inputs["L0_attn_Wo"].astype(f32),
        "w1in": inputs["L0_ffn_w1"].astype(f32),
        "w2in": inputs["L0_ffn_w2"].astype(f32),
        "b1c": inputs["L0_ffn_b1"].reshape(DFF // 128, 128).T.astype(f32).copy(),
        "b2c": inputs["L0_ffn_b2"].reshape(D // 128, 128).T.astype(f32).copy(),
        "alog": inputs["L0_attn_A_log"].reshape(H, 1).astype(f32),
        "dtbin": inputs["L0_attn_dt_bias"].reshape(H, 1).astype(f32),
        "normw": inputs["L0_attn_norm_w"].reshape(128, 1).astype(f32),
        "consts": consts_np(),
    }
    x = np.asarray(inputs["L0_attn_x"], f32)
    in_maps = []
    for c in range(8):
        m = dict(com)
        m["xin"] = np.ascontiguousarray(x[c // 4])
        sel = np.zeros((128, 4), f32)
        sel[:, c % 4] = 1.0
        m["selq"] = sel
        in_maps.append(m)
    return in_maps


def kernel(**inputs):
    in_maps = _prep_inputs(inputs)
    nc, _ = build_program()
    res = run_bass_kernel_spmd(nc, in_maps, core_ids=list(range(8)))
    out = np.empty((B, L, D), np.float32)
    for c in range(8):
        b, q = c // 4, c % 4
        out[b, QTOK * q:QTOK * (q + 1), :] = res.results[c]["out"]
    return out
